# version 09
# speedup vs baseline: 1.2098x; 1.0301x over previous
import math
from contextlib import ExitStack

import numpy as np
import concourse.bass as bass
import concourse.mybir as mybir
from concourse.bass_utils import run_bass_kernel_spmd

F32 = mybir.dt.float32
BF16 = mybir.dt.bfloat16
AF = mybir.ActivationFunctionType
ALU = mybir.AluOpType

D = 2048
S = 2048
B = 4
NH = 16
DK = 128
II = 8192
NT = D // 128
NI = II // 128
TOKF = 2048
TOKO = 1024
SDK = 1.0 / math.sqrt(DK)
NEG = -1.0e30
EPS1 = 1e-5
EPS2 = float(np.finfo(np.float32).eps)

ENGINES = ("sync", "scalar", "vector", "tensor", "gpsimd")
GP, DV, SC, TE, SY = "gpsimd", "vector", "scalar", "tensor", "sync"

R_RSTD, R_MUR, R_SCR = 0, 1024, 2048



class Prog:

    def __init__(self):
        self.q = {e: [] for e in ENGINES}
        self.cnt = {}
        self.selfsem = {}

    def sem(self, handle):
        self.cnt[handle] = 0
        return handle

    def set_self_sem(self, eng, handle):
        self.sem(handle)
        self.selfsem[eng] = handle

    def cur(self, sem):
        return self.cnt[sem]

    def do(self, eng, fn, wait=(), inc=(), auto=True):
        if auto and eng in self.selfsem:
            h = self.selfsem[eng]
            n = self.cnt[h]
            if n > 0:
                wait = tuple(wait) + ((h, n),)
            inc = tuple(inc) + ((h, 1),)
        for s, n in inc:
            self.cnt[s] += n
        wm = {}
        for s, n in wait:
            wm[s] = max(wm.get(s, 0), n)
        self.q[eng].append((fn, tuple(wm.items()), tuple(inc)))
        return self

    def dma(self, eng, out, in_, sem, wait=()):
        self.do(eng, lambda E, o=out, i=in_: E.dma_start(out=o, in_=i),
                wait=wait, inc=((sem, 16),), auto=False)
        return self.cnt[sem]

    def run(self, block):
        def make(eng_name):
            def body(E):
                for fn, waits, incs in self.q[eng_name]:
                    for s, v_ in waits:
                        E.wait_ge(s, v_)
                    inst = fn(E)
                    for s, n in incs:
                        inst.then_inc(s, n)
            return body
        for eng_name in ENGINES:
            if self.q[eng_name]:
                getattr(block, eng_name)(make(eng_name))


def alloc_common(nc, e, v, p, want_rope=True, want_ffn=True):
    v['bufA'] = e(nc.sbuf_tensor("bufA", [128, 32768], BF16))
    v['gbuf'] = [e(nc.sbuf_tensor(f"g{i}", [128, TOKF], BF16)) for i in range(2)]
    if want_rope:
        v['iscr'] = v['gbuf'][1][:, 0:1024].bitcast(mybir.dt.int32)
    v['at'] = e(nc.sbuf_tensor("at", [128, 4096], BF16))
    v['qh'] = e(nc.sbuf_tensor("qh", [128, TOKF], BF16))
    v['qr'] = e(nc.sbuf_tensor("qr", [128, TOKF], BF16))
    v['qlo'] = e(nc.sbuf_tensor("qlo", [128, TOKF], BF16))
    v['v_tm'] = e(nc.sbuf_tensor("v_tm", [128, TOKF], BF16))
    if want_rope:
        pass
    v['scr32'] = e(nc.sbuf_tensor("scr32", [128, TOKF], F32))
    v['xn_l'] = e(nc.sbuf_tensor("xn_l", [128, 8192], BF16))
    v['o_l'] = e(nc.sbuf_tensor("o_l", [128, 8192], BF16))
    if want_rope:
        xlf = v['xn_l'][:, :].bitcast(F32)
        v['cos_t'] = xlf[:, 0:TOKF]
        v['sin_t'] = xlf[:, TOKF:2 * TOKF]
        olf = v['o_l'][:, :].bitcast(F32)
        v['qr32'] = olf[:, 0:TOKF]
        v['qh32'] = olf[:, TOKF:2 * TOKF]
    v['wgb'] = [e(nc.sbuf_tensor(f"wgb{i}", [128, D], BF16)) for i in range(2)]
    v['wub'] = [e(nc.sbuf_tensor(f"wub{i}", [128, D], BF16)) for i in range(2)]
    v['wcb'] = [v['xn_l'][:, 0:2048], v['xn_l'][:, 2048:4096]]
    if want_ffn:
        v['wdb'] = [e(nc.sbuf_tensor(f"wdb{i}", [128, 4096], BF16)) for i in range(2)]
    v['wob_'] = [e(nc.sbuf_tensor(f"wocb{i}", [128, D], BF16)) for i in range(2)]
    v['rows'] = e(nc.sbuf_tensor("rows", [1, 4096], F32))
    v['cols'] = e(nc.sbuf_tensor("cols", [128, 312], F32))
    v['ones32'] = e(nc.sbuf_tensor("ones32", [128, 1], F32))
    v['onesr32'] = e(nc.sbuf_tensor("onesr32", [1, 128], F32))
    v['ones_bf'] = e(nc.sbuf_tensor("ones_bf", [128, 1], BF16))
    v['ident'] = e(nc.sbuf_tensor("ident", [128, 128], BF16))
    v['rt'] = e(nc.sbuf_tensor("rt", [128, 128], F32))
    v['tri'] = e(nc.sbuf_tensor("tri", [128, 128], F32))
    v['selc'] = e(nc.sbuf_tensor("selc", [128, 1], F32))
    v['dcol'] = e(nc.sbuf_tensor("dcol", [128, 1], F32))
    v['invf'] = e(nc.sbuf_tensor("invf", [128, 1], F32))
    v['cbias'] = e(nc.sbuf_tensor("cbias", [128, 4], F32))
    v['qob'] = e(nc.sbuf_tensor("qob", [128, TOKO], BF16))
    v['sgs'] = e(nc.sbuf_tensor("sgs", [128, 512], BF16))
    v['sgs32'] = e(nc.sbuf_tensor("sgs32", [128, 512], F32))
    if want_ffn:
        v['ge'] = [e(nc.sbuf_tensor(f"ge{j}", [128, 512], BF16)) for j in range(6)]
    v['kp'] = [v['v_tm'][:, 512:1024], v['v_tm'][:, 1024:1536]]
    v['sq4'] = [v['v_tm'][:, 0:512], v['v_tm'][:, 1536:2048],
                v['gbuf'][0][:, 0:512], v['gbuf'][0][:, 512:1024]]
    v['psS'] = e(nc.psum_tensor("psS", [128, 2048], F32))
    v['psA'] = e(nc.psum_tensor("psA", [128, 512], F32))
    v['psO'] = e(nc.psum_tensor("psO", [128, 512], F32))
    v['psT2'] = [e(nc.psum_tensor(f"psT{i}", [128, 1024], BF16))
                 for i in range(2)]
    for nm in ("sW", "sX", "sPE", "sAC", "sDV", "sPO", "sST"):
        v[nm] = p.sem(e(nc.semaphore(nm)))
    p.set_self_sem(DV, v['sDV'])
    p.set_self_sem(SC, v['sAC'])
    p.set_self_sem(GP, v['sPO'])
    for nm in ("sSmS", "sSmG", "sXc", "sXn", "sXo", "sXr",
               "sSTx", "sSTo", "sST1", "sCC", "sCi", "sCB", "sCW"):
        v[nm] = p.sem(e(nc.semaphore(nm)))
    v['sWg'] = [p.sem(e(nc.semaphore(f"sWg{i}"))) for i in range(2)]
    v['sWu'] = [p.sem(e(nc.semaphore(f"sWu{i}"))) for i in range(2)]
    v['sWd'] = [p.sem(e(nc.semaphore(f"sWd{i}"))) for i in range(2)]
    v['sWo'] = [p.sem(e(nc.semaphore(f"sWo{i}"))) for i in range(2)]
    v['sCw'] = [p.sem(e(nc.semaphore(f"sCw{i}"))) for i in range(2)]
    def xbf_sl(t):
        return v['o_l'][:, t * 512:(t + 1) * 512]
    def xsq_sl(t):
        if t < 4:
            return v['gbuf'][0][:, t * 512:(t + 1) * 512]
        if t < 8:
            return v['gbuf'][1][:, (t - 4) * 512:(t - 3) * 512]
        return v['at'][:, (t - 8) * 512:(t - 7) * 512]
    v['xbf_sl'] = xbf_sl
    v['xsq_sl'] = xsq_sl


def emit_prelude(p, v, want_rope):
    sW, sX, sPE, sAC, sDV, sPO = (v['sW'], v['sX'], v['sPE'], v['sAC'],
                                  v['sDV'], v['sPO'])
    cols, rows, psS = v['cols'], v['rows'], v['psS']

    p.dma(SY, v['selc'][:, :], v['sel_col'][:, :], v['sSmS'])
    v['_sel_ld'] = p.cur(v['sSmS'])

    p.do(GP, lambda E: E.memset(v['ident'][:, :], 1.0))
    p.do(GP, lambda E: E.affine_select(out=v['ident'][:, :], in_=v['ident'][:, :],
         compare_op=ALU.is_equal, fill=0.0, base=0, pattern=[[-1, 128]],
         channel_multiplier=1))
    p.do(GP, lambda E: E.memset(v['ones32'][:, :], 1.0))
    p.do(GP, lambda E: E.memset(v['onesr32'][:, :], 1.0))
    p.do(GP, lambda E: E.memset(v['ones_bf'][:, :], 1.0))
    p.do(GP, lambda E: E.memset(v['tri'][:, :], 0.0))
    p.do(GP, lambda E: E.affine_select(out=v['tri'][:, :], in_=v['tri'][:, :],
         compare_op=ALU.is_ge, fill=NEG, base=0, pattern=[[-1, 128]],
         channel_multiplier=1))
    p.do(GP, lambda E: E.memset(v['cbias'][:, 0:1], math.pi / 2.0))
    p.do(GP, lambda E: E.memset(v['cbias'][:, 1:2], EPS1))
    p.do(GP, lambda E: E.memset(v['cbias'][:, 2:3], EPS2))
    p.do(GP, lambda E: E.memset(v['cbias'][:, 3:4], 1e-6))
    v['_const_po'] = p.cur(sPO)
    p.do(SC, lambda E: E.nop(), wait=((sPO, v['_const_po']),))

    return


def emit_rope_tables(p, v):
    sW, sX, sPE, sAC, sDV, sPO = (v['sW'], v['sX'], v['sPE'], v['sAC'],
                                  v['sDV'], v['sPO'])
    cols, rows, psS = v['cols'], v['rows'], v['psS']

    rta = v['scr32'][:, 0:128]
    rtb = v['scr32'][:, 128:256]
    p.do(GP, lambda E: E.memset(rta, 1.0))
    p.do(GP, lambda E: E.affine_select(out=rta, in_=rta,
         compare_op=ALU.is_equal, fill=0.0, base=-64, pattern=[[1, 128]],
         channel_multiplier=-1))
    p.do(GP, lambda E: E.memset(rtb, 1.0))
    p.do(GP, lambda E: E.affine_select(out=rtb, in_=rtb,
         compare_op=ALU.is_equal, fill=0.0, base=-64, pattern=[[-1, 128]],
         channel_multiplier=1))
    p.do(DV, lambda E: E.tensor_sub(v['rt'][:, :], rta, rtb),
         wait=((sPO, p.cur(sPO)),))

    p.do(GP, lambda E: E.iota(v['dcol'][:, :], pattern=[[0, 1]], base=0,
         channel_multiplier=1, allow_small_or_imprecise_dtypes=True),
         )
    pm = p.cur(sPO)
    p.do(DV, lambda E: E.tensor_single_scalar(cols[:, 13:14], v['dcol'][:, :],
         64.0, ALU.is_ge), wait=((sPO, pm),))
    p.do(DV, lambda E: E.scalar_tensor_tensor(out=cols[:, 14:15],
         in0=cols[:, 13:14], scalar=-64.0, in1=v['dcol'][:, :],
         op0=ALU.mult, op1=ALU.add))
    p.do(SC, lambda E: E.activation(out=v['invf'][:, :], in_=cols[:, 14:15],
         func=AF.Exp, scale=-math.log(10000.0) / 64.0),
         wait=((sDV, p.cur(sDV)),))

    xl = p.dma(SY, rows[0:1, 2048:4096], v['pos_row'][:, :], v['sSmS'],
               wait=((sDV, p.cur(sDV)), (sAC, p.cur(sAC)), (sPE, p.cur(sPE))))
    v['_sel_ld'] = p.cur(v['sSmS'])
    for c in range(4):
        p.do(TE, lambda E, c=c: E.matmul(psS[:, c * 512:(c + 1) * 512],
             v['onesr32'][:, :], rows[0:1, 2048 + c * 512:2048 + (c + 1) * 512],
             start=True, stop=True),
             wait=((v['sSmS'], xl), (sAC, p.cur(sAC)), (sDV, p.cur(sDV)))
                  if c == 0 else (),
             inc=((sPE, 1),) if c == 3 else ())
    pe0 = p.cur(sPE)
    p.do(DV, lambda E: E.tensor_scalar(out=v['scr32'][:, :], in0=psS[:, :],
         scalar1=v['invf'][:, :], scalar2=None, op0=ALU.mult),
         wait=((sPE, pe0), (sDV, p.cur(sDV))))
    for c in range(4):
        sl = slice(c * 512, (c + 1) * 512)
        p.do(DV, lambda E, sl=sl: E.tensor_single_scalar(v['sgs32'][:, :],
             v['scr32'][:, sl], 1.0 / (2.0 * math.pi), ALU.mult))
        p.do(DV, lambda E: E.tensor_copy(v['iscr'][:, :], v['sgs32'][:, :]))
        p.do(DV, lambda E: E.tensor_copy(v['sgs32'][:, :], v['iscr'][:, :]))
        p.do(DV, lambda E, sl=sl: E.scalar_tensor_tensor(out=v['scr32'][:, sl],
             in0=v['sgs32'][:, :], scalar=-2.0 * math.pi, in1=v['scr32'][:, sl],
             op0=ALU.mult, op1=ALU.add))
        p.do(DV, lambda E, sl=sl: E.tensor_single_scalar(v['sgs32'][:, :],
             v['scr32'][:, sl], math.pi, ALU.is_gt))
        p.do(DV, lambda E, sl=sl: E.scalar_tensor_tensor(out=v['scr32'][:, sl],
             in0=v['sgs32'][:, :], scalar=-2.0 * math.pi, in1=v['scr32'][:, sl],
             op0=ALU.mult, op1=ALU.add))
    dv0 = p.cur(sDV)
    p.do(SC, lambda E: E.activation(out=v['sin_t'][:, :], in_=v['scr32'][:, :],
         func=AF.Sin), wait=((sDV, dv0), (v['sSTx'], p.cur(v['sSTx'])))) 
    p.do(SC, lambda E: E.activation(out=v['scr32'][:, :], in_=v['scr32'][:, :],
         func=AF.Abs))
    p.do(SC, lambda E: E.activation(out=v['cos_t'][:, :], in_=v['scr32'][:, :],
         func=AF.Sin, scale=-1.0, bias=v['cbias'][:, 0:1]))


def emit_ln(p, v, src_dram, make_xn2, xn2_dst, xn_own_dram, yn_full_dst=None,
            srcs=None, first_w=(), chunks=(0, 1, 2, 3)):
    sW, sX, sPE, sAC, sDV, sST = (v['sW'], v['sX'], v['sPE'], v['sAC'], v['sDV'],
                                  v['sST'])
    rows, psS, cols = v['rows'], v['psS'], v['cols']
    xbf_sl, xsq_sl = v['xbf_sl'], v['xsq_sl']

    for ci, c in enumerate(chunks):
        c0 = c * 512
        own = c0 < TOKO
        entry = (src_dram, c0) if srcs is None else srcs[c]
        blend = len(entry) == 3
        w0 = tuple(first_w) if ci == 0 else ()
        if not blend:
            sap, sc0 = entry
            p.do(GP, lambda E, sap=sap, sc0=sc0: E.dma_start(
                 out=v['o_l'].rearrange("p (t n) -> p t n", t=NT),
                 in_=sap[:, sc0:sc0 + 512].rearrange("(t p) n -> p t n", p=128)),
                 wait=((sDV, p.cur(sDV)), (sAC, p.cur(sAC))) + w0,
                 inc=((v['sXc'], 16),), auto=False)
            wl = p.cur(v['sXc'])
        else:
            t0ap, t1ap, sc0 = entry
            p.do(GP, lambda E, t0ap=t0ap, sc0=sc0: E.dma_start(
                 out=v['o_l'].rearrange("p (t n) -> p t n", t=NT),
                 in_=t0ap[:, sc0:sc0 + 512].rearrange("(t p) n -> p t n", p=128)),
                 wait=((sDV, p.cur(sDV)), (sAC, p.cur(sAC)),
                       (v['sCC'], v['_gath_cc'])) + w0,
                 inc=((v['sXc'], 16),), auto=False)
            p.do(GP, lambda E, t1ap=t1ap, sc0=sc0: E.dma_start(
                 out=v['xn_l'].rearrange("p (t n) -> p t n", t=NT),
                 in_=t1ap[:, sc0:sc0 + 512].rearrange("(t p) n -> p t n", p=128)),
                 wait=((sDV, p.cur(sDV)),),
                 inc=((v['sXc'], 16),), auto=False)
            wl = p.cur(v['sXc'])
            p.do(DV, lambda E: E.tensor_sub(v['xn_l'][:, :], v['xn_l'][:, :],
                 v['o_l'][:, :]),
                 wait=((v['sXc'], wl), (v['sSmS'], v['_sel_ld'])))
            p.do(DV, lambda E: E.tensor_scalar(out=v['xn_l'][:, :],
                 in0=v['xn_l'][:, :], scalar1=v['selc'][:, :], scalar2=None,
                 op0=ALU.mult))
            p.do(DV, lambda E: E.tensor_add(v['o_l'][:, :], v['o_l'][:, :],
                 v['xn_l'][:, :]))
        for t in range(NT):
            wsq = ()
            if t == 0:
                wsq = ((v['sXc'], wl),)
                if blend:
                    wsq = wsq + ((sDV, p.cur(sDV)),)
            p.do(SC, lambda E, t=t: E.activation(out=xsq_sl(t), in_=xbf_sl(t),
                 func=AF.Square), wait=wsq)
        ac0 = p.cur(sAC)
        for t in range(NT):
            p.do(TE, lambda E, t=t: E.matmul(psS[0:1, 0:512], v['ones_bf'][:, :],
                 xbf_sl(t), start=(t == 0), stop=(t == NT - 1)),
                 wait=((v['sXc'], wl), (sDV, p.cur(sDV))) if t == 0 else ())
        for t in range(NT):
            p.do(TE, lambda E, t=t: E.matmul(psS[0:1, 512:1024], v['ones_bf'][:, :],
                 xsq_sl(t), start=(t == 0), stop=(t == NT - 1)),
                 wait=((sAC, ac0),) if t == 0 else (),
                 inc=((sPE, 1),) if t == NT - 1 else ())
        pe1 = p.cur(sPE)
        r_mu = rows[0:1, 2048:2560]
        r_var = rows[0:1, 2560:3072]
        r_t2 = rows[0:1, 3072:3584]
        r_t3 = rows[0:1, 3584:4096]
        if own:
            r_rstd = rows[0:1, R_RSTD + c0:R_RSTD + c0 + 512]
            r_mur = rows[0:1, R_MUR + c0:R_MUR + c0 + 512]
        else:
            r_rstd = rows[0:1, 3072:3584]
            r_mur = rows[0:1, 2048:2560]
        p.do(DV, lambda E, r_mu=r_mu: E.tensor_single_scalar(r_mu, psS[0:1, 0:512],
             1.0 / D, ALU.mult), wait=((sPE, pe1),))
        p.do(DV, lambda E, r_var=r_var: E.tensor_single_scalar(r_var,
             psS[0:1, 512:1024], 1.0 / D, ALU.mult))
        p.do(DV, lambda E, r_mu=r_mu, r_t2=r_t2: E.tensor_mul(r_t2, r_mu, r_mu))
        p.do(DV, lambda E, r_var=r_var, r_t2=r_t2: E.tensor_sub(r_var, r_var, r_t2),
             )
        dvv = p.cur(sDV)
        p.do(SC, lambda E, r_var=r_var, r_t2=r_t2: E.activation(out=r_t2,
             in_=r_var, func=AF.Ln, bias=v['cbias'][0:1, 1:2]), wait=((sDV, dvv),))
        p.do(SC, lambda E, r_rstd=r_rstd, r_t2=r_t2: E.activation(out=r_rstd,
             in_=r_t2, func=AF.Exp, scale=-0.5))
        p.do(DV, lambda E, r_mur=r_mur, r_mu=r_mu, r_rstd=r_rstd: E.tensor_mul(
             r_mur, r_mu, r_rstd), wait=((sAC, p.cur(sAC)),))
        if make_xn2:
            p.do(DV, lambda E, r_var=r_var, r_t3=r_t3: E.tensor_single_scalar(
                 r_t3, r_var, EPS1, ALU.add))
            p.do(DV, lambda E, r_t3=r_t3: E.reciprocal(r_t3, r_t3))
            p.do(DV, lambda E, r_var=r_var, r_t3=r_t3: E.tensor_mul(r_t3, r_var,
                 r_t3))
            p.do(SC, lambda E, r_t3=r_t3: E.activation(out=r_t3, in_=r_t3,
                 func=AF.Ln, bias=v['cbias'][0:1, 2:3]), wait=((sDV, p.cur(sDV)),))
            p.do(SC, lambda E, r_t3=r_t3: E.activation(out=r_t3, in_=r_t3,
                 func=AF.Exp, scale=-0.5))
        acr = p.cur(sAC)
        p.do(TE, lambda E, r_rstd=r_rstd: E.matmul(psS[:, 0:512], v['onesr32'][:, :],
             r_rstd, start=True, stop=True),
             wait=((sAC, acr), (sDV, p.cur(sDV))))
        if make_xn2:
            p.do(TE, lambda E, r_mur=r_mur: E.matmul(psS[:, 512:1024],
                 v['onesr32'][:, :], r_mur, start=True, stop=True))
            p.do(TE, lambda E, r_t3=r_t3: E.matmul(psS[:, 1024:1536],
                 v['onesr32'][:, :], r_t3, start=True, stop=True),
                 inc=((sPE, 1),))
        else:
            p.do(TE, lambda E, r_mur=r_mur: E.matmul(psS[:, 512:1024],
                 v['onesr32'][:, :], r_mur, start=True, stop=True),
                 inc=((sPE, 1),))
        pe2 = p.cur(sPE)
        for t in range(NT):
            xs = xbf_sl(t)
            if yn_full_dst is not None:
                xn_dst = yn_full_dst[:, t * TOKF + c0:t * TOKF + c0 + 512]
            elif own:
                xn_dst = v['xn_l'][:, t * 512:(t + 1) * 512]
            else:
                xn_dst = xsq_sl(t)
            last = (t == NT - 1)
            p.do(DV, lambda E, xs=xs: E.tensor_mul(v['sgs32'][:, :], xs,
                 psS[:, 0:512]),
                 wait=((sPE, pe2), (v['sSTx'], p.cur(v['sSTx'])))
                      if t == 0 else ())
            p.do(DV, lambda E, xn_dst=xn_dst: E.tensor_sub(xn_dst, v['sgs32'][:, :],
                 psS[:, 512:1024]))
            if make_xn2:
                p.do(DV, lambda E, xn_dst=xn_dst: E.tensor_mul(v['sgs32'][:, :],
                     xn_dst, psS[:, 1024:1536]))
                p.do(DV, lambda E, t=t, c0=c0: E.tensor_scalar(
                     out=xn2_dst[:, t * TOKF + c0:t * TOKF + c0 + 512],
                     in0=v['sgs32'][:, :], scalar1=cols[:, 16 + t:17 + t],
                     scalar2=None, op0=ALU.mult),
                     wait=((v['sSmG'], v['_rms_ld']),)
                          if (t == 0 and c == 0) else ())
        if own and xn_own_dram is not None and yn_full_dst is None:
            p.dma(SY, xn_own_dram.rearrange("p (t n) -> p t n", t=NT)[:, :, c0:c0 + 512],
                  v['xn_l'].rearrange("p (t n) -> p t n", t=NT)[:, :, 0:512],
                  v['sSTx'], wait=((sDV, p.cur(sDV)),))


def emit_head_qproj(p, v, w_tiles, rhs_buf, bias_col, par, prev_pe_done,
                    bias_ld=0, n_chunks=4, out_buf=None, out_buf2=None):
    sPE, sAC, sDV = v['sPE'], v['sAC'], v['sDV']
    psS = v['psS']
    if out_buf is None:
        out_buf = v['qh']
    buf = v['wgb'][par % 2]
    sem = v['sWg'][par % 2]
    p.do(GP, lambda E, buf=buf: E.dma_start(out=buf[:, :], in_=w_tiles),
         wait=((sPE, prev_pe_done),) if prev_pe_done else (),
         inc=((sem, 16),), auto=False)
    wl = p.cur(sem)
    epc = 2 if out_buf2 is not None else 1
    for c in range(n_chunks):
        ps = psS[:, (c % 2) * 512:(c % 2) * 512 + 512]
        w0 = [(sem, wl), (sAC, p.cur(sAC)), (sDV, p.cur(sDV))] if c == 0 else \
             ([(sAC, p.cur(sAC) - epc)] if c >= 2 else [])
        for kt in range(NT):
            p.do(TE, lambda E, buf=buf, kt=kt, c=c, ps=ps: E.matmul(ps,
                 buf[:, kt * 128:(kt + 1) * 128],
                 rhs_buf[:, kt * TOKF + c * 512:kt * TOKF + c * 512 + 512],
                 start=(kt == 0), stop=(kt == NT - 1)),
                 wait=tuple(w0) if kt == 0 else (),
                 inc=((sPE, 1),) if kt == NT - 1 else ())
        pev = p.cur(sPE)
        p.do(SC, lambda E, c=c, ps=ps, bias_col=bias_col, out_buf=out_buf:
             E.activation(out=out_buf[:, c * 512:c * 512 + 512], in_=ps,
             func=AF.Identity, bias=bias_col),
             wait=((sPE, pev), (v['sSmG'], bias_ld)))
        if out_buf2 is not None:
            p.do(SC, lambda E, c=c, ps=ps, bias_col=bias_col, out_buf2=out_buf2:
                 E.activation(out=out_buf2[:, c * 512:c * 512 + 512], in_=ps,
                 func=AF.Identity, bias=bias_col))
    return p.cur(sPE), p.cur(sAC)


def emit_rope(p, v, src_ap, dst_ap):
    sPE, sAC, sDV = v['sPE'], v['sAC'], v['sDV']
    for c in range(4):
        c0 = c * 512
        p.do(TE, lambda E, c0=c0: E.matmul(v['psA'][:, :], v['rt'][:, :],
             src_ap[:, c0:c0 + 512], start=True, stop=True),
             wait=((sDV, p.cur(sDV)), (sAC, p.cur(sAC))), inc=((sPE, 1),))
        pe = p.cur(sPE)
        p.do(DV, lambda E, c0=c0: E.tensor_mul(v['sgs32'][:, :], v['psA'][:, :],
             v['sin_t'][:, c0:c0 + 512]), wait=((sPE, pe),))
        p.do(DV, lambda E, c0=c0: E.tensor_mul(v['scr32'][:, 0:512],
             src_ap[:, c0:c0 + 512], v['cos_t'][:, c0:c0 + 512]))
        p.do(DV, lambda E, c0=c0, dst_ap=dst_ap: E.tensor_add(
             dst_ap[:, c0:c0 + 512], v['sgs32'][:, :], v['scr32'][:, 0:512]))
    return p.cur(sDV)


def emit_attention_head(p, v, h, qr_ap, kr_ap, v_ap, causal, negc_done, o_out_fn,
                        split=None):
    sW, sX, sPE, sAC, sDV = v['sW'], v['sX'], v['sPE'], v['sAC'], v['sDV']
    psS, psA, psO = v['psS'], v['psA'], v['psO']
    cols, gbuf, at, v_tm = v['cols'], v['gbuf'], v['at'], v['v_tm']
    sel_ld = v['_sel_ld']

    for half in range(2):
        pst = v['psT2'][half % 2]
        for j in range(8):
            kt = half * 8 + j
            w = ()
            if j == 0:
                w = ((sDV, negc_done), (sAC, p.cur(sAC)))
            p.do(TE, lambda E, kt=kt, j=j, pst=pst: E.transpose(
                 pst[:, j * 128:(j + 1) * 128],
                 v_ap[:, kt * 128:(kt + 1) * 128],
                 v['ident'][:, :]), wait=w, inc=((sPE, 1),) if j == 7 else ())
        pv = p.cur(sPE)
        p.do(SC, lambda E, half=half, pst=pst: E.activation(
             out=v_tm[:, half * 1024:(half + 1) * 1024], in_=pst[:, 0:1024],
             func=AF.Identity), wait=((sPE, pv),))
    vtm_done = p.cur(sAC)
    cp_done = {}
    ghalf = [0]

    expA_prev = p.cur(sAC)
    expB_prev = expA_prev
    for g in range(4):
        for q2 in range(2):
            qt = g * 2 + q2
            qb = qt * 128

            def smm(k0, w0):
                if split is None:
                    p.do(TE, lambda E, k0=k0, qb=qb: E.matmul(
                         psS[:, k0:k0 + 512], qr_ap[:, qb:qb + 128],
                         kr_ap[:, k0:k0 + 512], start=True, stop=True),
                         wait=w0, inc=((sPE, 1),))
                else:
                    qhi, qlo = split
                    for i3, (lh, rh) in enumerate(((qhi, qhi), (qhi, qlo),
                                                   (qlo, qhi))):
                        p.do(TE, lambda E, k0=k0, qb=qb, lh=lh, rh=rh, i3=i3:
                             E.matmul(psS[:, k0:k0 + 512],
                             lh[:, qb:qb + 128], rh[:, k0:k0 + 512],
                             start=(i3 == 0), stop=(i3 == 2)),
                             wait=w0 if i3 == 0 else (),
                             inc=((sPE, 1),) if i3 == 2 else ())

            first = True
            for kc in (0, 1):
                k0 = kc * 512
                if causal and k0 > qb:
                    continue
                smm(k0, ((sAC, expA_prev), (sDV, p.cur(sDV))) if first else ())
                first = False
            pscA = p.cur(sPE)
            if causal:
                if qb + 128 < 1024:
                    p.do(DV, lambda E, qb=qb: E.memset(psS[:, qb + 128:1024],
                         NEG), wait=((sPE, pscA),))
                    w_tri = ()
                else:
                    w_tri = ((sPE, pscA),)
                p.do(DV, lambda E, qb=qb: E.tensor_add(psS[:, qb:qb + 128],
                     v['tri'][:, :], psS[:, qb:qb + 128]), wait=w_tri)
                mskA = ((sDV, p.cur(sDV)),)
            else:
                mskA = ((sPE, pscA), (sDV, p.cur(sDV)))
            p.do(SC, lambda E, q2=q2, qt=qt: E.activation(
                 out=gbuf[q2][:, 0:1024], in_=psS[:, 0:1024], func=AF.Exp,
                 scale=SDK, bias=cols[:, qt:qt + 1], accum_out=cols[:, 9:10]),
                 wait=mskA)
            expA_prev = p.cur(sAC)
            first = True
            for kc in (2, 3):
                smm(kc * 512, ((sAC, expB_prev), (sDV, p.cur(sDV)))
                    if first else ())
                first = False
            pscB = p.cur(sPE)
            p.do(SC, lambda E, q2=q2, qt=qt: E.activation(
                 out=gbuf[q2][:, 1024:2048], in_=psS[:, 1024:2048], func=AF.Exp,
                 scale=SDK, bias=cols[:, qt:qt + 1], accum_out=cols[:, 10:11]),
                 wait=((sPE, pscB), (sDV, p.cur(sDV))))
            expB_prev = p.cur(sAC)
            ed = p.cur(sAC)
            if causal:
                p.do(DV, lambda E: E.tensor_scalar(out=cols[:, 12:13],
                     in0=cols[:, 10:11], scalar1=v['selc'][:, :], scalar2=None,
                     op0=ALU.mult), wait=((sAC, ed), (v['sSmS'], sel_ld)))
                p.do(DV, lambda E: E.tensor_add(cols[:, 11:12], cols[:, 9:10],
                     cols[:, 12:13]))
            else:
                p.do(DV, lambda E: E.tensor_add(cols[:, 11:12], cols[:, 9:10],
                     cols[:, 10:11]), wait=((sAC, ed),))
            p.do(DV, lambda E: E.reciprocal(cols[:, 11:12], cols[:, 11:12]))
            p.do(DV, lambda E, q2=q2: E.tensor_scalar(out=gbuf[q2][:, 0:1024],
                 in0=gbuf[q2][:, 0:1024], scalar1=cols[:, 11:12], scalar2=None,
                 op0=ALU.mult))
            if causal:
                p.do(DV, lambda E: E.tensor_scalar(out=cols[:, 12:13],
                     in0=cols[:, 11:12], scalar1=v['selc'][:, :], scalar2=None,
                     op0=ALU.mult))
                rdB = cols[:, 12:13]
            else:
                rdB = cols[:, 11:12]
            p.do(DV, lambda E, q2=q2, rdB=rdB: E.tensor_scalar(
                 out=gbuf[q2][:, 1024:2048], in0=gbuf[q2][:, 1024:2048],
                 scalar1=rdB, scalar2=None, op0=ALU.mult))
        pres_done = p.cur(sDV)
        for half in range(4):
            gh = ghalf[0]
            pst = v['psT2'][gh % 2]
            for j in range(4):
                kt = half * 4 + j
                for q2 in range(2):
                    w = ()
                    if j == 0 and q2 == 0:
                        w = ((sAC, cp_done.get(gh - 2, p.cur(sAC))),
                             (sDV, pres_done))
                    p.do(TE, lambda E, kt=kt, j=j, q2=q2, pst=pst: E.transpose(
                         pst[:, (j * 2 + q2) * 128:(j * 2 + q2 + 1) * 128],
                         gbuf[q2][:, kt * 128:(kt + 1) * 128], v['ident'][:, :]),
                         wait=w, inc=((sPE, 1),) if (j == 3 and q2 == 1) else ())
            ptr = p.cur(sPE)
            p.do(SC, lambda E, half=half, pst=pst: E.activation(
                 out=at[:, half * 1024:(half + 1) * 1024], in_=pst[:, 0:1024],
                 func=AF.Identity), wait=((sPE, ptr),))
            cp_done[gh] = p.cur(sAC)
            ghalf[0] += 1
        at_ac = p.cur(sAC)
        for kt in range(NT):
            p.do(TE, lambda E, kt=kt: E.matmul(psO[:, 0:256],
                 v_tm[:, kt * 128:(kt + 1) * 128], at[:, kt * 256:(kt + 1) * 256],
                 start=(kt == 0), stop=(kt == NT - 1)),
                 wait=((sAC, at_ac), (sDV, p.cur(sDV))) if kt == 0 else (),
                 inc=((sPE, 1),) if kt == NT - 1 else ())
        o_out_fn(h, g, psO, p.cur(sPE))


def emit_l0_negc(p, v, qr_ap):
    sPE, sAC, sDV = v['sPE'], v['sAC'], v['sDV']
    p.do(SC, lambda E: E.activation(out=v['scr32'][:, 0:1024],
         in_=qr_ap[:, 0:1024], func=AF.Square),
         wait=((sDV, p.cur(sDV)), (sPE, p.cur(sPE))))
    sq_done = p.cur(sAC)
    for qt in range(8):
        p.do(TE, lambda E, qt=qt: E.matmul(v['psA'][:, qt:qt + 1],
             v['scr32'][:, qt * 128:(qt + 1) * 128], v['ones32'][:, :],
             start=True, stop=True),
             wait=((sAC, sq_done), (sDV, p.cur(sDV))) if qt == 0 else (),
             inc=((sPE, 1),) if qt == 7 else ())
    pcr = p.cur(sPE)
    p.do(DV, lambda E: E.tensor_single_scalar(v['cols'][:, 0:8],
         v['psA'][:, 0:8], -SDK, ALU.mult), wait=((sPE, pcr),))
    return p.cur(sDV)


def emit_negc_tail(p, v, crow, mult, crd):
    sPE, sDV = v['sPE'], v['sDV']
    ac0 = p.cur(v['sAC'])
    for qt in range(8):
        p.do(TE, lambda E, qt=qt, crow=crow: E.transpose(v['psA'][:, 0:1],
             crow[0:1, qt * 128:(qt + 1) * 128], v['onesr32'][0:1, 0:1]),
             wait=((sDV, crd if qt == 0 else p.cur(sDV)),) +
                  (((v['sAC'], ac0),) if qt == 0 else ()),
             inc=((sPE, 1),))
        pq = p.cur(sPE)
        p.do(DV, lambda E, qt=qt, mult=mult: E.tensor_single_scalar(
             v['cols'][:, qt:qt + 1], v['psA'][:, 0:1], mult, ALU.mult),
             wait=((sPE, pq),))
    return p.cur(sDV)


def emit_l1_negc(p, v, q_ap, k_ap):
    sPE, sAC, sDV = v['sPE'], v['sAC'], v['sDV']
    rows, at = v['rows'], v['at']
    p.do(SC, lambda E: E.activation(out=at[:, 0:2048], in_=k_ap[:, 0:TOKF],
         func=AF.Square), wait=((sDV, p.cur(sDV)), (sPE, p.cur(sPE))))
    sqk = p.cur(sAC)
    for c in range(4):
        p.do(TE, lambda E, c=c: E.matmul(v['psS'][0:1, c * 512:(c + 1) * 512],
             v['ones_bf'][:, :], at[:, c * 512:(c + 1) * 512],
             start=True, stop=True), wait=((sAC, sqk),) if c == 0 else (),
             inc=((sPE, 1),) if c == 3 else ())
    pk = p.cur(sPE)
    p.do(SC, lambda E: E.activation(out=rows[0:1, 2048:4096],
         in_=v['psS'][0:1, 0:2048], func=AF.Ln), wait=((sPE, pk),))
    ack = p.cur(sAC)
    p.do(DV, lambda E: E.reduce_max(v['cols'][0:1, 13:14], rows[0:1, 2048:4096],
         axis=mybir.AxisListType.X), wait=((sAC, ack),))
    p.do(DV, lambda E: E.tensor_scalar(out=v['cols'][0:1, 14:15],
         in0=v['cols'][0:1, 13:14], scalar1=0.5, scalar2=math.log(SDK),
         op0=ALU.mult, op1=ALU.add))
    dvb = p.cur(sDV)
    p.do(SC, lambda E: E.activation(out=at[:, 2048:3072], in_=q_ap[:, 0:TOKO],
         func=AF.Square), wait=((sDV, dvb),))
    sqq = p.cur(sAC)
    for qt in range(8):
        p.do(TE, lambda E, qt=qt: E.matmul(v['psA'][:, qt:qt + 1],
             at[:, 2048 + qt * 128:2048 + (qt + 1) * 128], v['ones_bf'][:, :],
             start=True, stop=True),
             wait=((sAC, sqq), (sDV, p.cur(sDV))) if qt == 0 else (),
             inc=((sPE, 1),) if qt == 7 else ())
    p.do(TE, lambda E: E.matmul(v['psA'][:, 8:9], v['onesr32'][:, :],
         v['cols'][0:1, 14:15], start=True, stop=True), inc=((sPE, 1),))
    pq_ = p.cur(sPE)
    p.do(DV, lambda E: E.tensor_copy(v['cols'][:, 15:16], v['psA'][:, 8:9]),
         wait=((sPE, pq_),))
    dvc = p.cur(sDV)
    p.do(SC, lambda E: E.activation(out=v['sgs32'][:, 0:8],
         in_=v['psA'][:, 0:8], func=AF.Ln), wait=((sDV, dvc),))
    p.do(SC, lambda E: E.activation(out=v['sgs32'][:, 8:16],
         in_=v['sgs32'][:, 0:8], func=AF.Exp, scale=0.5,
         bias=v['cols'][:, 15:16]))
    acq = p.cur(sAC)
    p.do(DV, lambda E: E.tensor_single_scalar(v['cols'][:, 0:8],
         v['sgs32'][:, 8:16], -1.0, ALU.mult), wait=((sAC, acq),))
    return p.cur(sDV)


def emit_l0_body(p, v, d):
    wq, wo, wg, wu, wd = d['wq'], d['wo'], d['wg'], d['wu'], d['wd']
    xn_d, o_d, x1o = d['xn_d'], d['o_d'], d['x1o']
    sW, sX, sPE, sAC, sDV, sST = (v['sW'], v['sX'], v['sPE'], v['sAC'],
                                  v['sDV'], v['sST'])

    if True:
        p.dma(GP, v['cols'][:, 16:16 + NT], d['rms_c'][:, :], v['sSmG'])
        bql = p.dma(GP, v['cols'][:, 32:32 + NT], d['wqb_c'][:, :], v['sSmG'])
        v['_rms_ld'] = bql

        emit_ln(p, v, d['x_fm'], make_xn2=True, xn2_dst=v['bufA'],
                xn_own_dram=xn_d)
        emit_rope_tables(p, v)

        def o_out(h, g, psO, pev):
            p.do(SC, lambda E, h=h, g=g: E.activation(out=v['sgs'][:, :256],
                 in_=psO[:, 0:256], func=AF.Identity),
                 wait=((sPE, pev), (v['sSTo'], p.cur(v['sSTo'])))),
            p.dma(SY, o_d[:, h * TOKO + g * 256:h * TOKO + (g + 1) * 256],
                  v['sgs'][:, :256], v['sSTo'], wait=((sAC, p.cur(sAC)),))

        prev_pe = 0
        for h in range(NH):
            prev_pe_w = prev_pe if h >= 2 else 0
            pe_d, ac_d = emit_head_qproj(p, v, wq[h], v['bufA'],
                                         v['cols'][:, 32 + h:33 + h], h,
                                         prev_pe_w, bias_ld=bql,
                                         out_buf=v['qh'], out_buf2=v['qh32'])
            prev_pe = pe_d
            emit_rope(p, v, v['qh32'], v['qr32'])
            p.do(SC, lambda E: E.activation(out=v['qr'][:, :],
                 in_=v['qr32'][:, :], func=AF.Identity),
                 wait=((sDV, p.cur(sDV)), (sPE, p.cur(sPE))))
            p.do(DV, lambda E: E.tensor_sub(v['qlo'][:, :], v['qr32'][:, :],
                 v['qr'][:, :]), wait=((sAC, p.cur(sAC)),))
            ndv = emit_l0_negc(p, v, v['qr32'])
            emit_attention_head(p, v, h, v['qr32'], v['qr32'], v['qh'],
                                causal=True, negc_done=ndv, o_out_fn=o_out,
                                split=(v['qr'], v['qlo']))
        attn_done = p.cur(sAC)

        bol = p.dma(GP, v['cols'][:, 48:48 + NT], d['wob_c'][:, :], v['sSmG'])
        for tc in range(2):
            t0 = tc * 512
            xnl = p.dma(SY, v['xn_l'][:, :],
                        xn_d.rearrange("p (t n) -> p t n", t=NT)[:, :, t0:t0 + 512],
                        v['sXn'], wait=((sPE, p.cur(sPE)), (sDV, p.cur(sDV)),
                                        (v['sSTx'], p.cur(v['sSTx']))))
            onl = p.dma(SY, v['o_l'][:, :],
                        o_d.rearrange("p (t n) -> p t n", t=NH)[:, :, t0:t0 + 512],
                        v['sXo'], wait=((v['sSTo'], p.cur(v['sSTo'])),
                                        (sAC, p.cur(sAC)), (sDV, p.cur(sDV)),
                                        (sPE, p.cur(sPE))))
            def sparsify_group(t):
                sl = [v['bufA'][:, (j * 16 + t) * 512:(j * 16 + t) * 512 + 512]
                      for j in range(4)]
                for j in range(4):
                    p.do(SC, lambda E, j=j, sl=sl: E.activation(
                         out=v['sq4'][j][:, :], in_=sl[j], func=AF.Square),
                         wait=((sDV, p.cur(sDV)),) if j == 0 else ())
                sqd = p.cur(sAC)
                GE, KP = v['ge'], v['kp']
                pairs = [(0, 1), (0, 2), (0, 3), (1, 2), (1, 3), (2, 3)]
                for i, (a, b_) in enumerate(pairs):
                    p.do(DV, lambda E, i=i, a=a, b_=b_: E.tensor_tensor(
                         GE[i][:, :], v['sq4'][a][:, :], v['sq4'][b_][:, :],
                         ALU.is_ge), wait=((sAC, sqd),) if i == 0 else ())
                p.do(DV, lambda E: E.tensor_add(KP[0][:, :], GE[0][:, :],
                     GE[1][:, :]))
                p.do(DV, lambda E: E.tensor_add(KP[0][:, :], KP[0][:, :],
                     GE[2][:, :]))
                p.do(DV, lambda E: E.tensor_single_scalar(KP[0][:, :],
                     KP[0][:, :], 1.5, ALU.is_ge))
                p.do(DV, lambda E, sl=sl: E.tensor_mul(sl[0], sl[0],
                     KP[0][:, :]))
                p.do(DV, lambda E: E.tensor_add(KP[1][:, :], GE[3][:, :],
                     GE[4][:, :]))
                p.do(DV, lambda E: E.tensor_sub(KP[1][:, :], KP[1][:, :],
                     GE[0][:, :]))
                p.do(DV, lambda E: E.tensor_single_scalar(KP[1][:, :],
                     KP[1][:, :], 0.5, ALU.is_ge))
                p.do(DV, lambda E, sl=sl: E.tensor_mul(sl[1], sl[1],
                     KP[1][:, :]))
                p.do(DV, lambda E: E.tensor_sub(KP[0][:, :], GE[5][:, :],
                     GE[1][:, :]))
                p.do(DV, lambda E: E.tensor_sub(KP[0][:, :], KP[0][:, :],
                     GE[3][:, :]))
                p.do(DV, lambda E: E.tensor_single_scalar(KP[0][:, :],
                     KP[0][:, :], -0.5, ALU.is_ge))
                p.do(DV, lambda E, sl=sl: E.tensor_mul(sl[2], sl[2],
                     KP[0][:, :]))
                p.do(DV, lambda E: E.tensor_add(KP[1][:, :], GE[2][:, :],
                     GE[4][:, :]))
                p.do(DV, lambda E: E.tensor_add(KP[1][:, :], KP[1][:, :],
                     GE[5][:, :]))
                p.do(DV, lambda E: E.tensor_single_scalar(KP[1][:, :],
                     KP[1][:, :], 1.5, ALU.is_lt))
                p.do(DV, lambda E, sl=sl: E.tensor_mul(sl[3], sl[3],
                     KP[1][:, :]))

            pe_done = {}
            for mt in range(NI):
                wgb = v['wgb'][mt % 2]
                wub = v['wub'][mt % 2]
                wt = [(sPE, pe_done[mt - 2])] if mt >= 2 else [(sPE, p.cur(sPE))]
                p.do(GP, lambda E, mt=mt, wgb=wgb: E.dma_start(out=wgb[:, :],
                     in_=wg[mt]), wait=tuple(wt), inc=((v['sWg'][mt % 2], 16),),
                     auto=False)
                p.do(GP, lambda E, mt=mt, wub=wub: E.dma_start(out=wub[:, :],
                     in_=wu[mt]), inc=((v['sWu'][mt % 2], 16),), auto=False)
                wlg = p.cur(v['sWg'][mt % 2])
                wlu = p.cur(v['sWu'][mt % 2])
                for kt in range(NT):
                    p.do(TE, lambda E, kt=kt, wgb=wgb: E.matmul(v['psS'][:, 0:512],
                         wgb[:, kt * 128:(kt + 1) * 128],
                         v['xn_l'][:, kt * 512:(kt + 1) * 512],
                         start=(kt == 0), stop=(kt == NT - 1)),
                         wait=((v['sWg'][mt % 2], wlg), (sAC, p.cur(sAC)),
                               (v['sXn'], xnl)) if kt == 0 else (),
                         inc=((sPE, 1),) if kt == NT - 1 else ())
                pg = p.cur(sPE)
                p.do(SC, lambda E: E.activation(out=v['sgs'][:, :],
                     in_=v['psS'][:, 0:512], func=AF.Silu),
                     wait=((sPE, pg), (sDV, p.cur(sDV)),
                           (v['sSTo'], p.cur(v['sSTo']))))
                sg = p.cur(sAC)
                for kt in range(NT):
                    p.do(TE, lambda E, kt=kt, wub=wub: E.matmul(v['psS'][:, 512:1024],
                         wub[:, kt * 128:(kt + 1) * 128],
                         v['xn_l'][:, kt * 512:(kt + 1) * 512],
                         start=(kt == 0), stop=(kt == NT - 1)),
                         wait=((sDV, p.cur(sDV)), (v['sWu'][mt % 2], wlu))
                              if kt == 0 else (),
                         inc=((sPE, 1),) if kt == NT - 1 else ())
                pu = p.cur(sPE)
                pe_done[mt] = pu
                p.do(DV, lambda E, mt=mt: E.tensor_mul(
                     v['bufA'][:, mt * 512:(mt + 1) * 512], v['sgs'][:, :],
                     v['psS'][:, 512:1024]), wait=((sPE, pu), (sAC, sg)),
                     )
                if mt >= 48:
                    sparsify_group(mt - 48)
            h_done = p.cur(sDV)
            pe_done_d = {}
            for dt in range(NT):
                wobuf = v['wob_'][dt % 2]
                wt0 = [(sPE, pe_done_d[dt - 2])] if dt >= 2 else \
                      [(sPE, p.cur(sPE))]
                wls = []
                for hf in range(2):
                    wdb = v['wdb'][hf]
                    wtH = [(sPE, pe_done_d[dt - 1])] if (dt >= 1 and hf == 0) \
                        else ([(sPE, p.cur(sPE))] if dt == 0 else [])
                    p.do(GP, lambda E, dt=dt, hf=hf, wdb=wdb: E.dma_start(
                         out=wdb[:, :], in_=wd[dt][:, hf * 4096:(hf + 1) * 4096]),
                         wait=tuple(wtH), inc=((v['sWd'][hf], 16),), auto=False)
                    wls.append(p.cur(v['sWd'][hf]))
                p.do(GP, lambda E, dt=dt, wobuf=wobuf: E.dma_start(out=wobuf[:, :],
                     in_=wo[dt]), wait=tuple(wt0), inc=((v['sWo'][dt % 2], 16),),
                     auto=False)
                wlo = p.cur(v['sWo'][dt % 2])
                for kt in range(NI):
                    wdb = v['wdb'][kt // 32]
                    w0 = ()
                    if kt == 0:
                        w0 = ((v['sWd'][0], wls[0]), (sDV, p.cur(sDV)))
                    elif kt == 32:
                        w0 = ((v['sWd'][1], wls[1]),)
                    p.do(TE, lambda E, kt=kt, wdb=wdb: E.matmul(v['psS'][:, 1024:1536],
                         wdb[:, (kt % 32) * 128:(kt % 32 + 1) * 128],
                         v['bufA'][:, kt * 512:(kt + 1) * 512],
                         start=(kt == 0), stop=(kt == NI - 1)), wait=w0)
                for kt in range(NT):
                    p.do(TE, lambda E, kt=kt, wobuf=wobuf: E.matmul(
                         v['psS'][:, 1536:2048], wobuf[:, kt * 128:(kt + 1) * 128],
                         v['o_l'][:, kt * 512:(kt + 1) * 512],
                         start=(kt == 0), stop=(kt == NT - 1)),
                         wait=((sAC, p.cur(sAC)), (v['sWo'][dt % 2], wlo),
                               (v['sXo'], onl)) if kt == 0 else (),
                         inc=((sPE, 1),) if kt == NT - 1 else ())
                pdp = p.cur(sPE)
                pe_done_d[dt] = pdp
                p.do(SC, lambda E, dt=dt: E.activation(out=v['sgs32'][:, :],
                     in_=v['psS'][:, 1536:2048], func=AF.Identity,
                     bias=v['cols'][:, 48 + dt:49 + dt]),
                     wait=((sPE, pdp), (v['sSmG'], bol), (sDV, p.cur(sDV))),
                     )
                acA = p.cur(sAC)
                x32 = v['scr32'][:, 0:512]
                scB = v['scr32'][:, 512:1024]
                scC = v['scr32'][:, 1024:1536]
                xl = p.dma(SY, x32, d['x_fm'][dt * 128:(dt + 1) * 128, t0:t0 + 512],
                           v['sXr'], wait=((sDV, p.cur(sDV)),))
                p.do(TE, lambda E: E.matmul(v['psA'][:, :], v['onesr32'][:, :],
                     v['rows'][0:1, R_RSTD + t0:R_RSTD + t0 + 512],
                     start=True, stop=True),
                     wait=((sAC, acA), (sDV, p.cur(sDV))))
                p.do(TE, lambda E: E.matmul(v['psO'][:, :], v['onesr32'][:, :],
                     v['rows'][0:1, R_MUR + t0:R_MUR + t0 + 512],
                     start=True, stop=True), inc=((sPE, 1),))
                pbb = p.cur(sPE)
                p.do(DV, lambda E, x32=x32, scB=scB: E.tensor_mul(scB, x32,
                     v['psA'][:, :]),
                     wait=((sPE, pbb), (v['sXr'], xl),
                           (v['sST1'], p.cur(v['sST1']))))
                p.do(DV, lambda E, scB=scB: E.tensor_sub(scB, scB, v['psO'][:, :]))
                p.do(DV, lambda E, scC=scC: E.tensor_add(scC,
                     v['psS'][:, 1024:1536], v['sgs32'][:, :]))
                p.do(DV, lambda E, scB=scB, scC=scC: E.tensor_add(scC, scC, scB))
                p.do(DV, lambda E, x32=x32, scC=scC: E.tensor_add(scC, scC, x32),
                     )
                if d.get('capture_xl') and tc == 1:
                    p.do(DV, lambda E, dt=dt, scC=scC: E.tensor_copy(
                         v['cols'][:, 288 + dt:289 + dt], scC[:, 511:512]))
                p.dma(SY, x1o[dt * 128:(dt + 1) * 128, t0:t0 + 512], scC,
                      v['sST1'], wait=((sDV, p.cur(sDV)),))




def emit_l1_body(p, v, d):
    wqkv, wproj, wgate = d['wqkv'], d['wproj'], d['wgate']
    o_d, g_d, x2o = d['o_d'], d['g_d'], d['x2o']
    sPE, sAC, sDV = v['sPE'], v['sAC'], v['sDV']
    sm_all = v['_l1b_ld']
    pe0 = p.cur(sPE)
    oxl = p.cur(v['sXo'])

    if True:
        gath1, gath0 = d['gath'][1], d['gath'][0]
        srcs = [(d['x1own'], 0), (d['x1own'], 512),
                (gath1, gath0, 0), (gath1, gath0, 512)]
        emit_ln(p, v, None, make_xn2=False, xn2_dst=None, xn_own_dram=None,
                yn_full_dst=v['bufA'], srcs=srcs, first_w=((sPE, pe0),),
                chunks=(0, 1))

        par_done = {}
        for mt in range(NT):
            buf = v['wub'][mt % 2]
            sem = v['sWu'][mt % 2]
            wt = ((sPE, par_done.get('g%d' % (mt - 2), pe0)),) \
                if mt >= 2 else ((sPE, pe0),)
            p.do(GP, lambda E, mt=mt, buf=buf: E.dma_start(out=buf[:, :],
                 in_=wgate[mt]), wait=wt, inc=((sem, 16),), auto=False)
            wl = p.cur(sem)
            for c in range(2):
                ps = v['psS'][:, (c % 2) * 512:(c % 2) * 512 + 512]
                w0 = [(sem, wl), (sAC, p.cur(sAC)), (sDV, p.cur(sDV))] \
                    if c == 0 else [(sAC, p.cur(sAC) - 1)] if (mt * 2 + c) >= 2 else []
                for kt in range(NT):
                    p.do(TE, lambda E, buf=buf, kt=kt, c=c, ps=ps: E.matmul(ps,
                         buf[:, kt * 128:(kt + 1) * 128],
                         v['bufA'][:, kt * TOKF + c * 512:kt * TOKF + c * 512 + 512],
                         start=(kt == 0), stop=(kt == NT - 1)),
                         wait=tuple(w0) if kt == 0 else (),
                         inc=((sPE, 1),) if kt == NT - 1 else ())
                pev = p.cur(sPE)
                p.do(SC, lambda E, mt=mt, c=c, ps=ps: E.activation(
                     out=v['sgs'][:, :], in_=ps, func=AF.Sigmoid,
                     bias=v['cols'][:, 224 + mt:225 + mt]),
                     wait=((sPE, pev), (v['sSTx'], p.cur(v['sSTx'])),
                           (v['sSTo'], p.cur(v['sSTo']))))
                p.dma(SY, g_d[:, mt * TOKO + c * 512:mt * TOKO + c * 512 + 512],
                      v['sgs'][:, :], v['sSTx'], wait=((sAC, p.cur(sAC)),))
            par_done['g%d' % mt] = p.cur(sPE)

        emit_ln(p, v, None, make_xn2=False, xn2_dst=None, xn_own_dram=None,
                yn_full_dst=v['bufA'], srcs=srcs, chunks=(2, 3))
        v['_xnl_free'] = p.cur(sDV)

        rounds = emit_chain_rounds(p, v, d)
        rounds[0][0]()
        rounds[0][1]()

        def o_out(h, g, psO, pev):
            p.do(SC, lambda E, h=h, g=g: E.activation(out=v['sgs'][:, :256],
                 in_=psO[:, 0:256], func=AF.Identity),
                 wait=((sPE, pev), (v['sSTo'], p.cur(v['sSTo']))))
            p.dma(SY, o_d[:, h * TOKO + g * 256:h * TOKO + (g + 1) * 256],
                  v['sgs'][:, :256], v['sSTo'],
                  wait=((sAC, p.cur(sAC)), (v['sXo'], oxl)))

        par = 0
        ri = 1
        for h in range(NH):
            w = par_done.get(par - 2, pe0)
            pe_d, _ = emit_head_qproj(p, v, wqkv[NT + h], v['bufA'],
                                      v['cols'][:, 192 + h:193 + h], par, w,
                                      bias_ld=sm_all, out_buf=v['qh'])
            par_done[par] = pe_d; par += 1
            w = par_done.get(par - 2, pe0)
            pe_d, _ = emit_head_qproj(p, v, wqkv[2 * NT + h], v['bufA'],
                                      v['cols'][:, 208 + h:209 + h], par, w,
                                      bias_ld=sm_all, out_buf=v['qr'])
            par_done[par] = pe_d; par += 1
            w = par_done.get(par - 2, pe0)
            pe_d, _ = emit_head_qproj(p, v, wqkv[h], v['bufA'],
                                      v['cols'][:, 176 + h:177 + h], par, w,
                                      bias_ld=sm_all, n_chunks=2, out_buf=v['qob'])
            par_done[par] = pe_d; par += 1
            if ri < len(rounds):
                rounds[ri][0]()
            ndv = emit_l1_negc(p, v, v['qob'], v['qh'])
            emit_attention_head(p, v, h, v['qob'], v['qh'], v['qr'],
                                causal=False, negc_done=ndv, o_out_fn=o_out)
            if ri < len(rounds):
                rounds[ri][1]()
                ri += 1

        for tc in range(2):
            t0 = tc * 512
            gl = p.dma(SY, v['xn_l'][:, :],
                       g_d.rearrange("p (t n) -> p t n", t=NT)[:, :, t0:t0 + 512],
                       v['sXn'], wait=((sPE, p.cur(sPE)), (sDV, p.cur(sDV)),
                                       (v['sSTx'], p.cur(v['sSTx']))))
            onl = p.dma(SY, v['o_l'][:, :],
                        o_d.rearrange("p (t n) -> p t n", t=NH)[:, :, t0:t0 + 512],
                        v['sXo'], wait=((v['sSTo'], p.cur(v['sSTo'])),
                                        (sAC, p.cur(sAC)), (sDV, p.cur(sDV))))
            for dt in range(NT):
                ya = v['bufA'][:, dt * TOKF + t0:dt * TOKF + t0 + 512]
                p.do(DV, lambda E, dt=dt: E.tensor_scalar(out=v['sgs'][:, :],
                     in0=v['xn_l'][:, dt * 512:(dt + 1) * 512],
                     scalar1=v['cols'][:, 272 + dt:273 + dt], scalar2=None,
                     op0=ALU.mult),
                     wait=((v['sXn'], gl), (sAC, p.cur(sAC)),
                           (v['sSTo'], p.cur(v['sSTo']))) if dt == 0 else ())
                p.do(DV, lambda E, ya=ya: E.tensor_add(ya, v['sgs'][:, :], ya))
                p.do(SC, lambda E, ya=ya: E.activation(out=v['sq4'][0], in_=ya,
                     func=AF.Square),
                     wait=((sDV, p.cur(sDV)), (sPE, p.cur(sPE))))
                acsq = p.cur(sAC)
                p.do(TE, lambda E, ya=ya, dt=dt: E.matmul(v['psS'][0:1, 0:512],
                     v['ones_bf'][:, :], ya, start=(dt == 0), stop=(dt == NT - 1)),
                     wait=((sAC, acsq),))
                p.do(TE, lambda E, dt=dt: E.matmul(v['psS'][0:1, 512:1024],
                     v['ones_bf'][:, :], v['sq4'][0], start=(dt == 0),
                     stop=(dt == NT - 1)), inc=((sPE, 1),))
            pst = p.cur(sPE)
            r_mu = v['rows'][0:1, 2048:2560]
            r_var = v['rows'][0:1, 2560:3072]
            r_t2 = v['rows'][0:1, 3072:3584]
            p.do(DV, lambda E: E.tensor_single_scalar(r_mu, v['psS'][0:1, 0:512],
                 1.0 / D, ALU.mult), wait=((sPE, pst),))
            p.do(DV, lambda E: E.tensor_single_scalar(r_var, v['psS'][0:1, 512:1024],
                 1.0 / D, ALU.mult))
            p.do(DV, lambda E: E.tensor_mul(r_t2, r_mu, r_mu))
            p.do(DV, lambda E: E.tensor_sub(r_var, r_var, r_t2))
            dvv = p.cur(sDV)
            p.do(SC, lambda E: E.activation(out=r_t2, in_=r_var, func=AF.Ln,
                 bias=v['cbias'][0:1, 3:4]), wait=((sDV, dvv),))
            p.do(SC, lambda E: E.activation(out=r_var, in_=r_t2, func=AF.Exp,
                 scale=-0.5))
            acr = p.cur(sAC)
            p.do(TE, lambda E: E.matmul(v['psA'][:, :], v['onesr32'][:, :], r_mu,
                 start=True, stop=True), wait=((sAC, acr), (sDV, p.cur(sDV))))
            p.do(TE, lambda E: E.matmul(v['psO'][:, :], v['onesr32'][:, :], r_var,
                 start=True, stop=True), inc=((sPE, 1),))
            pbb = p.cur(sPE)
            pe_done_d = {}
            for dt in range(NT):
                wobuf = v['wob_'][dt % 2]
                wt0 = [(sPE, pe_done_d[dt - 2])] if dt >= 2 else \
                      [(sPE, p.cur(sPE))]
                p.do(GP, lambda E, dt=dt, wobuf=wobuf: E.dma_start(out=wobuf[:, :],
                     in_=wproj[dt]), wait=tuple(wt0),
                     inc=((v['sWo'][dt % 2], 16),), auto=False)
                wlo = p.cur(v['sWo'][dt % 2])
                for kt in range(NT):
                    p.do(TE, lambda E, kt=kt, wobuf=wobuf: E.matmul(
                         v['psS'][:, 1536:2048], wobuf[:, kt * 128:(kt + 1) * 128],
                         v['o_l'][:, kt * 512:(kt + 1) * 512],
                         start=(kt == 0), stop=(kt == NT - 1)),
                         wait=((sAC, p.cur(sAC)), (v['sWo'][dt % 2], wlo),
                               (v['sXo'], onl), (sDV, p.cur(sDV)))
                              if kt == 0 else (),
                         inc=((sPE, 1),) if kt == NT - 1 else ())
                pdp = p.cur(sPE)
                pe_done_d[dt] = pdp
                p.do(SC, lambda E, dt=dt: E.activation(out=v['sgs32'][:, :],
                     in_=v['psS'][:, 1536:2048], func=AF.Identity,
                     bias=v['cols'][:, 160 + dt:161 + dt]),
                     wait=((sPE, pdp), (sDV, p.cur(sDV))))
                acA = p.cur(sAC)
                x32 = v['scr32'][:, 0:512]
                scB = v['scr32'][:, 512:1024]
                scC = v['scr32'][:, 1024:1536]
                xl = p.dma(SY, x32,
                           d['x1own'][dt * 128:(dt + 1) * 128, t0:t0 + 512],
                           v['sXr'], wait=((sDV, p.cur(sDV)),))
                ya = v['bufA'][:, dt * TOKF + t0:dt * TOKF + t0 + 512]
                p.do(DV, lambda E, ya=ya, scB=scB: E.tensor_sub(scB, ya,
                     v['psA'][:, :]),
                     wait=((sPE, pbb), (v['sXr'], xl), (sAC, acA),
                           (v['sST1'], p.cur(v['sST1']))))
                p.do(DV, lambda E, scB=scB: E.tensor_mul(scB, scB, v['psO'][:, :]))
                p.do(DV, lambda E, dt=dt, scB=scB: E.tensor_scalar(out=scB,
                     in0=scB, scalar1=v['cols'][:, 240 + dt:241 + dt],
                     scalar2=v['cols'][:, 256 + dt:257 + dt],
                     op0=ALU.mult, op1=ALU.add))
                p.do(DV, lambda E, scB=scB, scC=scC: E.tensor_add(scC,
                     v['sgs32'][:, :], scB))
                p.do(DV, lambda E, x32=x32, scC=scC: E.tensor_add(scC, scC, x32))
                p.dma(SY, x2o[dt * 128:(dt + 1) * 128, t0:t0 + 512], scC,
                      v['sST1'], wait=((sDV, p.cur(sDV)),))


CH_TH, CH_TZ, CH_W, CH_H, CH_CS, CH_SEL, CH_EV = 0, 64, 128, 192, 448, 456, 472
CH_ING, CH_INB, CH_PNG, CH_PNB = 512, 576, 640, 704
CH_AGGB, CH_D2B, CH_BCB, CH_D1B, CH_ZSRC = 768, 832, 896, 960, 1536


def emit_chain_preload(p, v, d):
    sCW = v['sCW']
    pe_l0 = p.cur(v['sPE'])
    st1 = p.cur(v['sST1'])
    chC = v['scr32']
    for mt in range(2):
        p.do(GP, lambda E, mt=mt: E.dma_start(
             out=v['wdb'][0][:, mt * 2048:(mt + 1) * 2048], in_=d['waggh'][mt]),
             wait=((v['sPE'], pe_l0),) if mt == 0 else (),
             inc=((sCW, 16),), auto=False)
        p.do(GP, lambda E, mt=mt: E.dma_start(
             out=v['wdb'][1][:, mt * 2048:(mt + 1) * 2048], in_=d['wbch'][mt]),
             inc=((sCW, 16),), auto=False)
    for off, key, n in ((CH_ING, 'ing4', 64), (CH_INB, 'inb4', 64),
                        (CH_PNG, 'png4', 64), (CH_PNB, 'pnb4', 64),
                        (CH_AGGB, 'aggb4', 64), (CH_D2B, 'd2b4', 64),
                        (CH_BCB, 'bcb4', 64), (CH_D1B, 'd1b4', 256)):
        p.do(GP, lambda E, off=off, key=key, n=n: E.dma_start(
             out=chC[:, off:off + n], in_=d[key][:, :]),
             wait=((v['sST1'], st1),) if off == CH_ING else (),
             inc=((sCW, 16),), auto=False)
    v['_ch_wl'] = v['_ch_bl'] = p.cur(sCW)


def emit_chain_rounds(p, v, d):
    sPE, sAC, sDV = v['sPE'], v['sAC'], v['sDV']
    sCW, sCB, sCi, sCC = v['sCW'], v['sCB'], v['sCi'], v['sCC']
    chC = v['scr32']
    Cbz = v['ge'][0][:, 0:64]
    Cbth = v['ge'][0][:, 64:128]
    Cbh = v['ge'][1][:, 0:256]
    psA, psO = v['psA'], v['psO']
    rw = v['rows']
    ev = chC[:, CH_EV:CH_EV + 32]
    AG = "AllGather"
    WORLD = [list(range(8))]
    cw_hist = {}
    cw_n = [0]

    def r0_a():
        for j in range(4):
            p.do(GP, lambda E, j=j: E.dma_start(
                 out=chC[:, CH_ZSRC + j * 16:CH_ZSRC + (j + 1) * 16],
                 in_=d['xl_all'][2 * j + 1]),
                 wait=((sCC, v['_xl_cc']), (sCW, v['_ch_bl']))
                 if j == 0 else (),
                 inc=((sCB, 16),), auto=False)

    def r0_b():
        zl = p.cur(sCB)
        tzv = chC[:, CH_TZ:CH_TZ + 64].rearrange("p (t j) -> p t j", j=4)
        for j in range(4):
            p.do(DV, lambda E, j=j, tzv=tzv: E.tensor_copy(tzv[:, :, j],
                 chC[:, CH_ZSRC + j * 16:CH_ZSRC + (j + 1) * 16]),
                 wait=((sCB, zl),) if j == 0 else ())
        col_ln_b(CH_TZ, CH_ING, CH_INB, Cbz,
                 wait0=((sCW, v['_ch_bl']),))

    def col_ln_b(src_off, g_off, b_off, out_bf_sl, wait0=()):
        sv = chC[:, src_off:src_off + 64].rearrange("p (t j) -> p t j", j=4)
        for b in range(4):
            p.do(DV, lambda E, b=b, sv=sv: E.reduce_sum(
                 chC[:, CH_CS + b:CH_CS + b + 1], sv[:, :, b],
                 axis=mybir.AxisListType.X), wait=tuple(wait0) if b == 0 else ())
        p.do(SC, lambda E: E.activation(out=chC[:, CH_W:CH_W + 64],
             in_=chC[:, src_off:src_off + 64], func=AF.Square),
             wait=((sDV, p.cur(sDV)),))
        sqv = chC[:, CH_W:CH_W + 64].rearrange("p (t j) -> p t j", j=4)
        for b in range(4):
            p.do(DV, lambda E, b=b, sqv=sqv: E.reduce_sum(
                 chC[:, CH_CS + 4 + b:CH_CS + 5 + b], sqv[:, :, b],
                 axis=mybir.AxisListType.X),
                 wait=((sAC, p.cur(sAC)),) if b == 0 else ())
        dvr = p.cur(sDV)
        p.do(TE, lambda E: E.matmul(psO[0:1, 0:8], v['ones32'][:, :],
             chC[:, CH_CS:CH_CS + 8], start=True, stop=True),
             wait=((sDV, dvr), (sAC, p.cur(sAC))), inc=((sPE, 1),))
        pe1 = p.cur(sPE)
        p.do(DV, lambda E: E.tensor_single_scalar(rw[0:1, 0:8],
             psO[0:1, 0:8], 1.0 / D, ALU.mult), wait=((sPE, pe1),))
        p.do(DV, lambda E: E.tensor_mul(rw[0:1, 12:16], rw[0:1, 0:4],
             rw[0:1, 0:4]))
        p.do(DV, lambda E: E.tensor_sub(rw[0:1, 8:12], rw[0:1, 4:8],
             rw[0:1, 12:16]))
        dvv = p.cur(sDV)
        p.do(SC, lambda E: E.activation(out=rw[0:1, 12:16],
             in_=rw[0:1, 8:12], func=AF.Ln, bias=v['cbias'][0:1, 3:4]),
             wait=((sDV, dvv),))
        p.do(SC, lambda E: E.activation(out=rw[0:1, 8:12],
             in_=rw[0:1, 12:16], func=AF.Exp, scale=-0.5))
        acx = p.cur(sAC)
        p.do(TE, lambda E: E.matmul(psO[:, 16:20], v['onesr32'][:, :],
             rw[0:1, 0:4], start=True, stop=True),
             wait=((sAC, acx), (sDV, p.cur(sDV))))
        p.do(TE, lambda E: E.matmul(psO[:, 20:24], v['onesr32'][:, :],
             rw[0:1, 8:12], start=True, stop=True), inc=((sPE, 1),))
        pe2 = p.cur(sPE)
        ov = chC[:, CH_W:CH_W + 64]
        ovv = ov.rearrange("p (t j) -> p t j", j=4)
        for b in range(4):
            p.do(DV, lambda E, b=b, sv=sv, ovv=ovv: E.tensor_scalar(
                 out=ovv[:, :, b], in0=sv[:, :, b],
                 scalar1=psO[:, 16 + b:17 + b], scalar2=psO[:, 20 + b:21 + b],
                 op0=ALU.subtract, op1=ALU.mult),
                 wait=((sPE, pe2),) if b == 0 else ())
        p.do(DV, lambda E, ov=ov, g_off=g_off: E.tensor_mul(ov, ov,
             chC[:, g_off:g_off + 64]))
        p.do(DV, lambda E, ov=ov, b_off=b_off: E.tensor_add(ov, ov,
             chC[:, b_off:b_off + 64]))
        p.do(DV, lambda E, ov=ov, out_bf_sl=out_bf_sl: E.tensor_copy(
             out_bf_sl, ov))

    def ag_tail(n_loc, cin_t, ag_t, dst_off, pe):
        p.do(SC, lambda E, n_loc=n_loc: E.activation(out=ev[:, 0:n_loc * 4],
             in_=psA[:, 0:n_loc * 4], func=AF.Identity),
             wait=((sPE, pe), (sCi, p.cur(sCi))))
        p.dma(SY, cin_t.ap()[:, :], ev[:, 0:n_loc * 4], sCi,
              wait=((sAC, p.cur(sAC)),))
        cil = p.cur(sCi)
        p.do(GP, lambda E: E.collective_compute(AG, ALU.bypass,
             replica_groups=WORLD, ins=[cin_t.ap().opt()],
             outs=[ag_t.ap().opt()]), wait=((sCi, cil),),
             inc=((sCC, 1),), auto=False)
        cc = p.cur(sCC)
        agp = ag_t.ap()
        for r in range(8):
            p.do(GP, lambda E, r=r, n_loc=n_loc, agp=agp, dst_off=dst_off:
                 E.dma_start(out=chC[:, dst_off + r * n_loc * 4:
                                     dst_off + (r + 1) * n_loc * 4],
                             in_=agp[r]),
                 wait=((sCC, cc),) if r == 0 else (),
                 inc=((sCB, 16),), auto=False)

    def gemm_res(buf, n_kt, rhs, cin_t, ag_t, dst_off):
        w0 = ((sCW, v['_ch_wl']), (sDV, p.cur(sDV)), (sAC, p.cur(sAC)))
        for mt in range(2):
            for kt in range(n_kt):
                p.do(TE, lambda E, mt=mt, kt=kt, buf=buf, rhs=rhs: E.matmul(
                     psA[:, mt * 4:(mt + 1) * 4],
                     buf[:, mt * 2048 + kt * 128:mt * 2048 + (kt + 1) * 128],
                     rhs[:, kt * 4:(kt + 1) * 4],
                     start=(kt == 0), stop=(kt == n_kt - 1)),
                     wait=w0 if (mt == 0 and kt == 0) else (),
                     inc=((sPE, 1),) if (mt == 1 and kt == n_kt - 1) else ())
        ag_tail(2, cin_t, ag_t, dst_off, p.cur(sPE))

    def cw_load(src_ap):
        pp = cw_n[0] % 2
        wt = ((sPE, cw_hist[cw_n[0] - 2]),) if cw_n[0] >= 2 else \
             ((sDV, v['_xnl_free']),)
        p.do(GP, lambda E, pp=pp, src_ap=src_ap: E.dma_start(
             out=v['wcb'][pp][:, :], in_=src_ap),
             wait=wt, inc=((v['sCw'][pp], 16),), auto=False)
        cw_n[0] += 1
        return pp, p.cur(v['sCw'][pp])

    def gemm_d1(s):
        w0 = ((sDV, p.cur(sDV)), (sAC, p.cur(sAC)))
        for mt in range(8):
            pp, wl = cw_load(d['wd1h'][mt])
            for kt in range(16):
                p.do(TE, lambda E, mt=mt, kt=kt, pp=pp: E.matmul(
                     psA[:, mt * 4:(mt + 1) * 4],
                     v['wcb'][pp][:, kt * 128:(kt + 1) * 128],
                     Cbz[:, kt * 4:(kt + 1) * 4],
                     start=(kt == 0), stop=(kt == 15)),
                     wait=((v['sCw'][pp], wl),) +
                          (w0 if (mt == 0 and kt == 0) else ())
                          if kt == 0 else (),
                     inc=((sPE, 1),) if kt == 15 else ())
            cw_hist[cw_n[0] - 1] = p.cur(sPE)
        ag_tail(8, d['cin_t'][f'd1_{s}'], d['ag_t'][f'd1_{s}'], CH_H,
                p.cur(sPE))

    def gemm_d2(s):
        w0 = ((sDV, p.cur(sDV)), (sAC, p.cur(sAC)))
        for mt in range(2):
            for ck in range(4):
                pp, wl = cw_load(d['wd2h'][mt][:, ck * 2048:(ck + 1) * 2048])
                for q in range(16):
                    p.do(TE, lambda E, mt=mt, ck=ck, q=q, pp=pp: E.matmul(
                         psA[:, mt * 4:(mt + 1) * 4],
                         v['wcb'][pp][:, q * 128:(q + 1) * 128],
                         Cbh[:, (ck * 16 + q) * 4:(ck * 16 + q + 1) * 4],
                         start=(ck == 0 and q == 0), stop=(ck == 3 and q == 15)),
                         wait=((v['sCw'][pp], wl),) +
                              (w0 if (mt == 0 and ck == 0 and q == 0) else ())
                              if q == 0 else (),
                         inc=((sPE, 1),) if q == 15 else ())
                cw_hist[cw_n[0] - 1] = p.cur(sPE)
        ag_tail(2, d['cin_t'][f'd2_{s}'], d['ag_t'][f'd2_{s}'], CH_W,
                p.cur(sPE))

    def agg_a():
        gemm_res(v['wdb'][0], 16, Cbz, d['cin_t']['agg'], d['ag_t']['agg'],
                 CH_W)

    def agg_b():
        p.do(DV, lambda E: E.tensor_add(chC[:, CH_TH:CH_TH + 64],
             chC[:, CH_W:CH_W + 64], chC[:, CH_AGGB:CH_AGGB + 64]),
             wait=((sCB, p.cur(sCB)),))

    def d1_a(s):
        col_ln_b(CH_TH, CH_PNG, CH_PNB, Cbz)
        gemm_d1(s)

    def d1_b(s):
        p.do(DV, lambda E: E.tensor_add(chC[:, CH_H:CH_H + 256],
             chC[:, CH_H:CH_H + 256], chC[:, CH_D1B:CH_D1B + 256]),
             wait=((sCB, p.cur(sCB)),))
        p.do(SC, lambda E: E.activation(out=Cbh, in_=chC[:, CH_H:CH_H + 256],
             func=AF.Gelu), wait=((sDV, p.cur(sDV)),))

    def d2_a(s):
        gemm_d2(s)

    def d2_b(s):
        p.do(DV, lambda E: E.tensor_add(chC[:, CH_W:CH_W + 64],
             chC[:, CH_W:CH_W + 64], chC[:, CH_D2B:CH_D2B + 64]),
             wait=((sCB, p.cur(sCB)),))
        p.do(DV, lambda E: E.tensor_add(chC[:, CH_TH:CH_TH + 64],
             chC[:, CH_TH:CH_TH + 64], chC[:, CH_W:CH_W + 64]))

    def bc_a():
        p.do(DV, lambda E: E.tensor_copy(Cbth, chC[:, CH_TH:CH_TH + 64]))
        gemm_res(v['wdb'][1], 16, Cbth, d['cin_t']['bc'], d['ag_t']['bc'],
                 CH_W)

    def bc_b():
        p.do(DV, lambda E: E.tensor_add(chC[:, CH_W:CH_W + 64],
             chC[:, CH_W:CH_W + 64], chC[:, CH_BCB:CH_BCB + 64]),
             wait=((sCB, p.cur(sCB)),))
        bv = chC[:, CH_W:CH_W + 64].rearrange("p (t j) -> p t j", j=4)
        for b in range(4):
            if b == 0:
                p.do(DV, lambda E, b=b, bv=bv: E.tensor_scalar(
                     out=v['cols'][:, 272:288], in0=bv[:, :, b],
                     scalar1=v['cols'][:, 304 + b:305 + b], scalar2=None,
                     op0=ALU.mult), wait=((v['sSmG'], v['_bsel_ld']),))
            else:
                p.do(DV, lambda E, b=b, bv=bv: E.tensor_scalar(
                     out=chC[:, CH_SEL:CH_SEL + 16], in0=bv[:, :, b],
                     scalar1=v['cols'][:, 304 + b:305 + b], scalar2=None,
                     op0=ALU.mult))
                p.do(DV, lambda E: E.tensor_add(v['cols'][:, 272:288],
                     v['cols'][:, 272:288], chC[:, CH_SEL:CH_SEL + 16]))

    rounds = [(r0_a, r0_b), (agg_a, agg_b)]
    for s in range(4):
        rounds.append((lambda s=s: d1_a(s), lambda s=s: d1_b(s)))
        rounds.append((lambda s=s: d2_a(s), lambda s=s: d2_b(s)))
    rounds.append((bc_a, bc_b))
    return rounds



def build_fused():
    nc = bass.Bass("TRN2", target_bir_lowering=False)
    ap = lambda *a, **k: nc.dram_tensor(*a, **k).ap()
    v = {}
    v['pos_row'] = ap("pos_row", [1, TOKF], F32, kind="ExternalInput")
    v['sel_col'] = ap("sel_col", [128, 1], F32, kind="ExternalInput")
    d = {}
    for nm, shp in (("x_fm", [D, TOKF]),
                    ("wq", [NT, 128, D]), ("wo", [NT, 128, D]),
                    ("wg", [NI, 128, D]), ("wu", [NI, 128, D]),
                    ("wd", [NT, 128, II]),
                    ("rms_c", [128, NT]), ("wqb_c", [128, NT]),
                    ("wob_c", [128, NT]),
                    ("wqkv", [3 * NT, 128, D]), ("wproj", [NT, 128, D]),
                    ("wgate", [NT, 128, D]),
                    ("qkvb_c", [128, 3 * NT]), ("projb_c", [128, NT]),
                    ("gateb_c", [128, NT]), ("outg_c", [128, NT]),
                    ("outb_c", [128, NT]), ("bsel", [128, 4]),
                    ("wd1h", [8, 128, D]), ("wd2h", [2, 128, II]),
                    ("waggh", [2, 128, D]), ("wbch", [2, 128, D]),
                    ("ing4", [128, 64]), ("inb4", [128, 64]),
                    ("png4", [128, 64]), ("pnb4", [128, 64]),
                    ("aggb4", [128, 64]), ("d2b4", [128, 64]),
                    ("bcb4", [128, 64]), ("d1b4", [128, 256])):
        d[nm] = ap(nm, shp, F32, kind="ExternalInput")
    d['x2o'] = ap("x2o", [D, TOKO], F32, kind="ExternalOutput")
    d['xn_d'] = ap("xn_d", [128, NT * TOKO], BF16, kind="Internal")
    d['o_d'] = ap("o_d", [128, NH * TOKO], BF16, kind="Internal")
    d['g_d'] = ap("g_d", [128, NT * TOKO], BF16, kind="Internal")
    x1own_t = nc.dram_tensor("x1own", [D, TOKO], F32, kind="Internal")
    gath_t = nc.dram_tensor("gath", [2, D, TOKO], F32, kind="Internal")
    xlc_t = nc.dram_tensor("xl_cin", [128, NT], F32, kind="Internal")
    xla_t = nc.dram_tensor("xl_all", [8, 128, NT], F32, kind="Internal")
    d['x1own'] = x1own_t.ap()
    d['gath'] = gath_t.ap()
    d['xl_all'] = xla_t.ap()
    d['x1o'] = d['x1own']
    d['capture_xl'] = True
    d['cin_t'] = {}
    d['ag_t'] = {}
    for nm, n in (("agg", 8), ("bc", 8), ("d1_0", 32), ("d1_1", 32),
                  ("d1_2", 32), ("d1_3", 32), ("d2_0", 8), ("d2_1", 8),
                  ("d2_2", 8), ("d2_3", 8)):
        d['cin_t'][nm] = nc.dram_tensor(f"cin_{nm}", [128, n], F32,
                                        kind="Internal")
        d['ag_t'][nm] = nc.dram_tensor(f"ag_{nm}", [8, 128, n], F32,
                                       kind="Internal")

    with ExitStack() as st:
        e = st.enter_context
        p = Prog()
        alloc_common(nc, e, v, p)
        emit_prelude(p, v, want_rope=True)
        for dst, key in ((v['cols'][:, 160:176], 'projb_c'),
                         (v['cols'][:, 176:224], 'qkvb_c'),
                         (v['cols'][:, 224:240], 'gateb_c'),
                         (v['cols'][:, 240:256], 'outg_c'),
                         (v['cols'][:, 256:272], 'outb_c'),
                         (v['cols'][:, 304:308], 'bsel')):
            p.dma(GP, dst, d[key][:, :], v['sSmG'])

        emit_l0_body(p, v, d)
        v['_l1b_ld'] = v['_bsel_ld'] = v['_rms_ld']

        emit_chain_preload(p, v, d)

        cil = p.dma(SY, xlc_t.ap()[:, :], v['cols'][:, 288:304], v['sCi'],
                    wait=((v['sDV'], p.cur(v['sDV'])),))
        st1 = p.cur(v['sST1'])
        p.do(GP, lambda E: E.collective_compute("AllGather", ALU.bypass,
             replica_groups=[list(range(8))],
             ins=[xlc_t.ap().opt()], outs=[xla_t.ap().opt()]),
             wait=((v['sCi'], cil), (v['sST1'], st1)),
             inc=((v['sCC'], 1),), auto=False)
        v['_xl_cc'] = p.cur(v['sCC'])
        p.do(GP, lambda E: E.collective_compute("AllGather", ALU.bypass,
             replica_groups=[[0, 1], [2, 3], [4, 5], [6, 7]],
             ins=[x1own_t.ap().opt()], outs=[gath_t.ap().opt()]),
             wait=((v['sST1'], st1),), inc=((v['sCC'], 1),), auto=False)
        v['_gath_cc'] = p.cur(v['sCC'])

        emit_l1_body(p, v, d)

        allsems = [v[nm] for nm in ("sST1", "sSTx", "sSTo", "sSmS", "sSmG",
                                    "sXc", "sXn", "sXo", "sXr",
                                    "sCC", "sCi", "sCB", "sCW")] + \
            v['sWg'] + v['sWu'] + v['sWd'] + v['sWo'] + v['sCw']
        p.do(SY, lambda E: E.nop(),
             wait=tuple((s_, p.cur(s_)) for s_ in allsems))

        block = e(nc.Block())
        p.run(block)
    return nc



def tile_lhsT_from_out_in(W):
    O, K = W.shape
    return np.ascontiguousarray(
        W.reshape(O // 128, 128, K // 128, 128).transpose(0, 3, 2, 1)
    ).reshape(O // 128, 128, K)


def tile_lhsT_from_in_out(W):
    K, M = W.shape
    return np.ascontiguousarray(
        W.reshape(K // 128, 128, M // 128, 128).transpose(2, 1, 0, 3)
    ).reshape(M // 128, 128, K)


def vcols(vv):
    n = vv.shape[0]
    return np.ascontiguousarray(vv.reshape(n // 128, 128).T)


def perm24(I=II):
    g = np.arange(I // 4)
    out = np.empty(I, np.int64)
    for j in range(4):
        out[j * (I // 4) + g] = g * 4 + j
    return out


def core_perm(h):
    T0 = h * TOKO
    if h == 0:
        return np.arange(S)
    return np.concatenate([np.arange(T0, T0 + TOKO), np.arange(0, T0)])


def stage_fused(inp):
    f32 = np.float32
    pm = perm24()
    g = lambda k: np.asarray(inp[k], f32)
    rep4 = lambda a: np.ascontiguousarray(np.repeat(a, 4, axis=1))
    sh = {
        "wq": tile_lhsT_from_out_in(g("L0_attn_Wq_w")),
        "wo": tile_lhsT_from_out_in(g("L0_attn_Wo_w")),
        "wg": tile_lhsT_from_in_out(g("L0_ffn_w_gate")[:, pm]),
        "wu": tile_lhsT_from_in_out(g("L0_ffn_w_up")[:, pm]),
        "wd": tile_lhsT_from_in_out(g("L0_ffn_w_down")[pm, :]),
        "rms_c": vcols(g("L0_attn_rms_w")),
        "wqb_c": vcols(g("L0_attn_Wq_b")),
        "wob_c": vcols(g("L0_attn_Wo_b")),
        "wqkv": tile_lhsT_from_out_in(g("L1_attn_Wqkv_w")),
        "wproj": tile_lhsT_from_out_in(g("L1_attn_proj_w")),
        "wgate": tile_lhsT_from_in_out(g("L1_ffn_gate_w")),
        "qkvb_c": vcols(g("L1_attn_Wqkv_b")),
        "projb_c": vcols(g("L1_attn_proj_b")),
        "gateb_c": vcols(g("L1_ffn_gate_b")),
        "outg_c": vcols(g("L1_ffn_out_gamma")),
        "outb_c": vcols(g("L1_ffn_out_beta")),
        "ing4": rep4(vcols(g("L1_ffn_in_gamma"))),
        "inb4": rep4(vcols(g("L1_ffn_in_beta"))),
        "png4": rep4(vcols(g("L1_ffn_pn_gamma"))),
        "pnb4": rep4(vcols(g("L1_ffn_pn_beta"))),
        "aggb4": rep4(vcols(g("L1_ffn_agg_b"))),
        "d2b4": rep4(vcols(g("L1_ffn_d2_b"))),
        "bcb4": rep4(vcols(g("L1_ffn_bc_b"))),
        "d1b4": rep4(vcols(g("L1_ffn_d1_b"))),
    }
    chain_w = {
        "_wagg": tile_lhsT_from_in_out(g("L1_ffn_agg_w")),
        "_wd1": tile_lhsT_from_in_out(g("L1_ffn_d1_w")),
        "_wd2": tile_lhsT_from_in_out(g("L1_ffn_d2_w")),
        "_wbc": tile_lhsT_from_in_out(g("L1_ffn_bc_w")),
    }
    return sh, chain_w


def fused_in_maps(inp, sh, chain_w):
    f32 = np.float32
    x = np.asarray(inp["L0_attn_X"], f32)
    in_maps = []
    for c in range(8):
        b, h = c // 2, c % 2
        perm = core_perm(h)
        bs = np.zeros((128, 4), f32)
        bs[:, b] = 1.0
        m = {
            "x_fm": np.ascontiguousarray(x[b].T[:, perm]),
            "pos_row": perm.astype(f32)[None, :].copy(),
            "sel_col": np.full((128, 1), float(h), f32),
            "bsel": bs,
            "waggh": np.ascontiguousarray(chain_w["_wagg"][2 * c:2 * c + 2]),
            "wbch": np.ascontiguousarray(chain_w["_wbc"][2 * c:2 * c + 2]),
            "wd1h": np.ascontiguousarray(chain_w["_wd1"][8 * c:8 * c + 8]),
            "wd2h": np.ascontiguousarray(chain_w["_wd2"][2 * c:2 * c + 2]),
        }
        m.update(sh)
        in_maps.append(m)
    return in_maps


def kernel(**inp):
    sh, chain_w = stage_fused(inp)
    in_maps = fused_in_maps(inp, sh, chain_w)
    nc = build_fused()
    res = run_bass_kernel_spmd(nc, in_maps, core_ids=list(range(8)))
    out = np.empty((B, S, D), np.float32)
    for c in range(8):
        b, h = c // 2, c % 2
        out[b, h * TOKO:(h + 1) * TOKO, :] = res.results[c]["x2o"].T
    return out
